# version 01 (final)
# baseline (speedup 1.0000x reference)
import numpy as np
import concourse.bass as bass
import concourse.mybir as mybir
from concourse.bass_utils import run_bass_kernel_spmd
from contextlib import ExitStack

F32 = mybir.dt.float32
AF = mybir.ActivationFunctionType
ALU = mybir.AluOpType
AX = mybir.AxisListType

NCORES = 8
N = 16384
D = 1024
TPC = N // NCORES
H, DK, DV = 16, 64, 64
FF = 4096
E = 8
CH = 64
NCHUNK = 8192 // CH


class Em:
    def __init__(self):
        self.ops = []
        self.cnt = {'pe': 0, 've': 0, 'sc': 0, 'dma': 0, 'dout': 0}
        self.last = None

    def _add(self, eng, key, f, inc):
        self.ops.append((eng, self.last, f, (key, inc)))
        self.cnt[key] += inc
        self.last = (key, self.cnt[key])

    def pe(self, f): self._add('tensor', 'pe', f, 1)
    def ve(self, f): self._add('vector', 've', f, 1)
    def sc(self, f): self._add('scalar', 'sc', f, 1)
    def dma(self, f): self._add('sync', 'dma', f, 16)
    def dmao(self, f): self._add('sync', 'dout', f, 16)

    def fin(self):
        self.ops.append(('sync', ('dout', self.cnt['dout']), None, None))

    def build(self, nc, stack):
        sems = {k: stack.enter_context(nc.semaphore(name=f"sem_{k}")) for k in self.cnt}
        blk = stack.enter_context(nc.Block())
        def mk(engname):
            def emitall(eng):
                for (en, wait, f, inc) in self.ops:
                    if en != engname:
                        continue
                    if wait is not None:
                        eng.wait_ge(sems[wait[0]], wait[1])
                    ins = f(eng) if f is not None else None
                    if inc is not None:
                        ins.then_inc(sems[inc[0]], inc[1])
            return emitall
        blk.sync(mk('sync'))
        blk.vector(mk('vector'))
        blk.scalar(mk('scalar'))
        blk.tensor(mk('tensor'))


def _bcast_row(em, nc, psb, onesk1, row_ap, out_sb, width):
    for h0 in range(0, width, 512):
        w = min(512, width - h0)
        em.pe(lambda e, a=psb[:, :w], b=onesk1[:, :], c=row_ap[:, h0:h0+w]:
              e.matmul(out=a, lhsT=b, rhs=c, start=True, stop=True))
        em.ve(lambda e, a=out_sb[:, h0:h0+w], b=psb[:, :w]: e.tensor_copy(out=a, in_=b))


def build_L1():
    nc = bass.Bass("TRN2", target_bir_lowering=False)
    st = ExitStack()
    x = nc.dram_tensor("x", [TPC, D], F32, kind="ExternalInput").ap()
    w = nc.dram_tensor("w", [1, D], F32, kind="ExternalInput").ap()
    o1 = nc.dram_tensor("o1", [1, 128], F32, kind="ExternalInput").ap()
    ep = nc.dram_tensor("ep", [128, 2], F32, kind="ExternalInput").ap()
    xn = nc.dram_tensor("xn", [TPC, D], F32, kind="ExternalOutput").ap()
    g = nc.dram_tensor("g", [TPC, D], F32, kind="ExternalOutput").ap()
    em = Em()
    ept = None
    xt = st.enter_context(nc.sbuf_tensor("xt", [128, D], F32))
    wb = st.enter_context(nc.sbuf_tensor("wb", [128, D], F32))
    wrow = st.enter_context(nc.sbuf_tensor("wrow", [1, D], F32))
    ones = st.enter_context(nc.sbuf_tensor("ones", [1, 128], F32))
    sq = st.enter_context(nc.sbuf_tensor("sq", [128, D], F32))
    xm = st.enter_context(nc.sbuf_tensor("xm", [128, D], F32))
    xnt = st.enter_context(nc.sbuf_tensor("xnt", [128, D], F32))
    gt = st.enter_context(nc.sbuf_tensor("gt", [128, D], F32))
    m1 = st.enter_context(nc.sbuf_tensor("m1", [128, 1], F32))
    m2 = st.enter_context(nc.sbuf_tensor("m2", [128, 1], F32))
    m3 = st.enter_context(nc.sbuf_tensor("m3", [128, 1], F32))
    ept = st.enter_context(nc.sbuf_tensor("ept", [128, 2], F32))
    psb = st.enter_context(nc.psum_tensor("psb", [128, 512], F32))
    em.dma(lambda e: e.dma_start(out=wrow[:, :], in_=w[:, :]))
    em.dma(lambda e: e.dma_start(out=ones[:, :], in_=o1[:, :]))
    em.dma(lambda e: e.dma_start(out=ept[:, :], in_=ep[:, :]))
    _bcast_row(em, nc, psb, ones, wrow, wb, D)
    for i in range(TPC // 128):
        r0 = i * 128
        em.dma(lambda e, r=r0: e.dma_start(out=xt[:, :], in_=x[r:r+128, :]))
        em.ve(lambda e: e.reduce_sum(out=m1[:, :], in_=xt[:, :], axis=AX.X))
        em.sc(lambda e: e.mul(out=m1[:, :], in_=m1[:, :], mul=-1.0 / D))
        em.ve(lambda e: e.tensor_scalar_add(out=xm[:, :], in0=xt[:, :], scalar1=m1[:, :]))
        em.ve(lambda e: e.tensor_mul(out=sq[:, :], in0=xm[:, :], in1=xm[:, :]))
        em.ve(lambda e: e.reduce_sum(out=m2[:, :], in_=sq[:, :], axis=AX.X))
        em.sc(lambda e: e.mul(out=m2[:, :], in_=m2[:, :], mul=1.0 / D))
        em.ve(lambda e: e.tensor_scalar_add(out=m2[:, :], in0=m2[:, :], scalar1=ept[:, 0:1]))
        em.sc(lambda e: e.activation(out=m2[:, :], in_=m2[:, :], func=AF.Sqrt))
        em.ve(lambda e: e.reciprocal(out=m2[:, :], in_=m2[:, :]))
        em.ve(lambda e: e.tensor_scalar_mul(out=xnt[:, :], in0=xm[:, :], scalar1=m2[:, :]))
        em.dmao(lambda e, r=r0: e.dma_start(out=xn[r:r+128, :], in_=xnt[:, :]))
        em.ve(lambda e: e.tensor_mul(out=sq[:, :], in0=xnt[:, :], in1=xnt[:, :]))
        em.ve(lambda e: e.reduce_sum(out=m3[:, :], in_=sq[:, :], axis=AX.X))
        em.sc(lambda e: e.mul(out=m3[:, :], in_=m3[:, :], mul=1.0 / D))
        em.ve(lambda e: e.tensor_scalar_add(out=m3[:, :], in0=m3[:, :], scalar1=ept[:, 1:2]))
        em.sc(lambda e: e.activation(out=m3[:, :], in_=m3[:, :], func=AF.Sqrt))
        em.ve(lambda e: e.reciprocal(out=m3[:, :], in_=m3[:, :]))
        em.ve(lambda e: e.tensor_scalar_mul(out=gt[:, :], in0=xnt[:, :], scalar1=m3[:, :]))
        em.ve(lambda e: e.tensor_mul(out=gt[:, :], in0=gt[:, :], in1=wb[:, :]))
        em.dmao(lambda e, r=r0: e.dma_start(out=g[r:r+128, :], in_=gt[:, :]))
    em.fin()
    em.build(nc, st)
    return nc


def build_L2():
    T = 8192
    nc = bass.Bass("TRN2", target_bir_lowering=False)
    st = ExitStack()
    g = nc.dram_tensor("g", [128, N], F32, kind="ExternalInput").ap()
    cw = nc.dram_tensor("cw", [128, 4], F32, kind="ExternalInput").ap()
    cb = nc.dram_tensor("cb", [128, 1], F32, kind="ExternalInput").ap()
    h = nc.dram_tensor("h", [128, N], F32, kind="ExternalOutput").ap()
    em = Em()
    gs = st.enter_context(nc.sbuf_tensor("gs", [128, T], F32))
    t0 = st.enter_context(nc.sbuf_tensor("t0", [128, T], F32))
    t1 = st.enter_context(nc.sbuf_tensor("t1", [128, T], F32))
    t2 = st.enter_context(nc.sbuf_tensor("t2", [128, T], F32))
    t3 = st.enter_context(nc.sbuf_tensor("t3", [128, T], F32))
    hc = st.enter_context(nc.sbuf_tensor("hc", [128, T], F32))
    cwt = st.enter_context(nc.sbuf_tensor("cwt", [128, 4], F32))
    cbt = st.enter_context(nc.sbuf_tensor("cbt", [128, 1], F32))
    em.dma(lambda e: e.dma_start(out=cwt[:, :], in_=cw[:, :]))
    em.dma(lambda e: e.dma_start(out=cbt[:, :], in_=cb[:, :]))
    for s in range(2):
        c0 = s * T
        em.dma(lambda e, c=c0: e.dma_start(out=gs[:, :], in_=g[:, c:c+T]))
        for j, tb in enumerate([t0, t1, t2, t3]):
            em.ve(lambda e, tb=tb, j=j: e.tensor_scalar_mul(out=tb[:, :], in0=gs[:, :], scalar1=cwt[:, j:j+1]))
        em.ve(lambda e: e.tensor_add(out=hc[:, 3:T], in0=t3[:, 3:T], in1=t2[:, 2:T-1]))
        em.ve(lambda e: e.tensor_add(out=hc[:, 3:T], in0=hc[:, 3:T], in1=t1[:, 1:T-2]))
        em.ve(lambda e: e.tensor_add(out=hc[:, 3:T], in0=hc[:, 3:T], in1=t0[:, 0:T-3]))
        em.ve(lambda e: e.tensor_copy(out=hc[:, 0:1], in_=t3[:, 0:1]))
        em.ve(lambda e: e.tensor_add(out=hc[:, 1:2], in0=t3[:, 1:2], in1=t2[:, 0:1]))
        em.ve(lambda e: e.tensor_add(out=hc[:, 2:3], in0=t3[:, 2:3], in1=t2[:, 1:2]))
        em.ve(lambda e: e.tensor_add(out=hc[:, 2:3], in0=hc[:, 2:3], in1=t1[:, 0:1]))
        em.ve(lambda e: e.tensor_scalar_add(out=hc[:, :], in0=hc[:, :], scalar1=cbt[:, :]))
        em.sc(lambda e: e.activation(out=t0[:, :], in_=hc[:, :], func=AF.Silu))
        em.dmao(lambda e, c=c0: e.dma_start(out=h[:, c:c+T], in_=t0[:, :]))
    em.fin()
    em.build(nc, st)
    return nc


def build_L3():
    nc = bass.Bass("TRN2", target_bir_lowering=False)
    st = ExitStack()
    ap_h = nc.dram_tensor("h", [D, TPC], F32, kind="ExternalInput").ap()
    ap_xn = nc.dram_tensor("xn", [D, TPC], F32, kind="ExternalInput").ap()
    ws = {}
    for nm in ["qw", "kw", "vw", "aw"]:
        ws[nm] = nc.dram_tensor(nm, [128, 8 * D], F32, kind="ExternalInput").ap()
    bw = nc.dram_tensor("bw", [128, 8 * 16], F32, kind="ExternalInput").ap()
    bb = nc.dram_tensor("bb", [16, 1], F32, kind="ExternalInput").ap()
    ab = nc.dram_tensor("ab", [128, 8], F32, kind="ExternalInput").ap()
    u1w = nc.dram_tensor("u1w", [128, 8 * 64], F32, kind="ExternalInput").ap()
    u1b = nc.dram_tensor("u1b", [64, 1], F32, kind="ExternalInput").ap()
    u2w = nc.dram_tensor("u2w", [64, D], F32, kind="ExternalInput").ap()
    u2b = nc.dram_tensor("u2b", [128, 8], F32, kind="ExternalInput").ap()
    gw = nc.dram_tensor("gw", [128, 8 * E], F32, kind="ExternalInput").ap()
    b2 = nc.dram_tensor("b2", [128, 2], F32, kind="ExternalInput").ap()
    i2 = nc.dram_tensor("i2", [2, 128], F32, kind="ExternalInput").ap()
    ep = nc.dram_tensor("ep", [128, 2], F32, kind="ExternalInput").ap()
    outs = {}
    for nm in ["qn", "kn", "vv", "aa", "gt"]:
        outs[nm] = nc.dram_tensor(nm, [D, TPC], F32, kind="ExternalOutput").ap()
    bt_o = nc.dram_tensor("bt", [16, TPC], F32, kind="ExternalOutput").ap()
    la_o = nc.dram_tensor("la", [TPC, E], F32, kind="ExternalOutput").ap()
    em = Em()
    hs = [st.enter_context(nc.sbuf_tensor(f"hs{i}", [128, TPC], F32)) for i in range(8)]
    xs = [st.enter_context(nc.sbuf_tensor(f"xs{i}", [128, TPC], F32)) for i in range(8)]
    wbuf = st.enter_context(nc.sbuf_tensor("wbuf", [128, 8 * D], F32))
    b2t = st.enter_context(nc.sbuf_tensor("b2t", [128, 2], F32))
    i2t = st.enter_context(nc.sbuf_tensor("i2t", [2, 128], F32))
    abt = st.enter_context(nc.sbuf_tensor("abt", [128, 8], F32))
    u2bt = st.enter_context(nc.sbuf_tensor("u2bt", [128, 8], F32))
    bbt = st.enter_context(nc.sbuf_tensor("bbt", [16, 1], F32))
    u1bt = st.enter_context(nc.sbuf_tensor("u1bt", [64, 1], F32))
    sqt = st.enter_context(nc.sbuf_tensor("sqt", [128, 512], F32))
    qs = st.enter_context(nc.sbuf_tensor("qs", [128, 512], F32))
    rq = st.enter_context(nc.sbuf_tensor("rq", [2, 512], F32))
    ot = st.enter_context(nc.sbuf_tensor("ot", [128, 512], F32))
    t1s = st.enter_context(nc.sbuf_tensor("t1s", [64, TPC], F32))
    ps = st.enter_context(nc.psum_tensor("ps", [128, 512], F32))
    ps2 = st.enter_context(nc.psum_tensor("ps2", [2, 512], F32))
    ps3 = st.enter_context(nc.psum_tensor("ps3", [128, 512], F32))
    pb = st.enter_context(nc.psum_tensor("pb", [64, 512], F32))
    pl = st.enter_context(nc.psum_tensor("pl", [128, 8], F32))
    for i in range(8):
        em.dma(lambda e, i=i: e.dma_start(out=hs[i][:, :], in_=ap_h[i*128:(i+1)*128, :]))
        em.dma(lambda e, i=i: e.dma_start(out=xs[i][:, :], in_=ap_xn[i*128:(i+1)*128, :]))
    em.dma(lambda e: e.dma_start(out=b2t[:, :], in_=b2[:, :]))
    em.dma(lambda e: e.dma_start(out=i2t[:, :], in_=i2[:, :]))
    em.dma(lambda e: e.dma_start(out=abt[:, :], in_=ab[:, :]))
    em.dma(lambda e: e.dma_start(out=u2bt[:, :], in_=u2b[:, :]))
    em.dma(lambda e: e.dma_start(out=bbt[:, :], in_=bb[:, :]))
    em.dma(lambda e: e.dma_start(out=u1bt[:, :], in_=u1b[:, :]))
    ept = st.enter_context(nc.sbuf_tensor("ept", [128, 2], F32))
    em.dma(lambda e: e.dma_start(out=ept[:, :], in_=ep[:, :]))
    NT = TPC // 512

    def proj_big(wname, oname, kind):
        em.dma(lambda e, w=ws[wname]: e.dma_start(out=wbuf[:, :], in_=w[:, :]))
        for fc in range(8):
            for tt in range(NT):
                for dc in range(8):
                    em.pe(lambda e, fc=fc, tt=tt, dc=dc: e.matmul(
                        out=ps[:, :], lhsT=wbuf[:, dc*D + fc*128: dc*D + (fc+1)*128],
                        rhs=hs[dc][:, tt*512:(tt+1)*512], start=(dc == 0), stop=(dc == 7)))
                if kind == "norm":
                    em.ve(lambda e: e.tensor_copy(out=qs[:, :], in_=ps[:, :]))
                    em.ve(lambda e: e.tensor_mul(out=sqt[:, :], in0=qs[:, :], in1=qs[:, :]))
                    em.pe(lambda e: e.matmul(out=ps2[:, :], lhsT=b2t[:, :], rhs=sqt[:, :],
                                             start=True, stop=True))
                    em.sc(lambda e: e.activation(out=rq[:, :], in_=ps2[:, :], func=AF.Sqrt))
                    em.ve(lambda e: e.tensor_scalar_add(out=rq[:, :], in0=rq[:, :],
                                                        scalar1=ept[0:2, 1:2]))
                    em.ve(lambda e: e.reciprocal(out=rq[:, :], in_=rq[:, :]))
                    em.pe(lambda e: e.matmul(out=ps3[:, :], lhsT=i2t[:, :], rhs=rq[:, :],
                                             start=True, stop=True))
                    em.ve(lambda e: e.tensor_mul(out=ot[:, :], in0=qs[:, :], in1=ps3[:, :]))
                elif kind == "sig":
                    em.sc(lambda e, fc=fc: e.activation(out=ot[:, :], in_=ps[:, :],
                          func=AF.Sigmoid, bias=abt[:, fc:fc+1], scale=1.0))
                else:
                    em.ve(lambda e: e.tensor_copy(out=ot[:, :], in_=ps[:, :]))
                em.dmao(lambda e, fc=fc, tt=tt, o=outs[oname]: e.dma_start(
                    out=o[fc*128:(fc+1)*128, tt*512:(tt+1)*512], in_=ot[:, :]))

    proj_big("qw", "qn", "norm")
    proj_big("kw", "kn", "norm")
    proj_big("vw", "vv", "copy")
    proj_big("aw", "aa", "sig")
    em.dma(lambda e: e.dma_start(out=wbuf[:, :8*16], in_=bw[:, :]))
    for tt in range(NT):
        for dc in range(8):
            em.pe(lambda e, tt=tt, dc=dc: e.matmul(
                out=pl[:16, :]if False else ps[:16, :512], lhsT=wbuf[:, dc*16:(dc+1)*16],
                rhs=hs[dc][:, tt*512:(tt+1)*512], start=(dc == 0), stop=(dc == 7)))
        em.sc(lambda e: e.activation(out=ot[:16, :], in_=ps[:16, :512], func=AF.Sigmoid,
                                     bias=bbt[:, :], scale=1.0))
        em.dmao(lambda e, tt=tt: e.dma_start(out=bt_o[:, tt*512:(tt+1)*512], in_=ot[:16, :]))
    em.dma(lambda e: e.dma_start(out=wbuf[:, :8*64], in_=u1w[:, :]))
    for tt in range(NT):
        for dc in range(8):
            em.pe(lambda e, tt=tt, dc=dc: e.matmul(
                out=pb[:, :], lhsT=wbuf[:, dc*64:(dc+1)*64],
                rhs=xs[dc][:, tt*512:(tt+1)*512], start=(dc == 0), stop=(dc == 7)))
        em.sc(lambda e, tt=tt: e.activation(out=t1s[:, tt*512:(tt+1)*512], in_=pb[:, :],
                                            func=AF.Silu, bias=u1bt[:, :], scale=1.0))
    em.dma(lambda e: e.dma_start(out=wbuf[:64, :D], in_=u2w[:, :]))
    for fc in range(8):
        for tt in range(NT):
            em.pe(lambda e, fc=fc, tt=tt: e.matmul(
                out=ps[:, :], lhsT=wbuf[:64, fc*128:(fc+1)*128],
                rhs=t1s[:, tt*512:(tt+1)*512], start=True, stop=True))
            em.sc(lambda e, fc=fc: e.activation(out=ot[:, :], in_=ps[:, :], func=AF.Sigmoid,
                                                bias=u2bt[:, fc:fc+1], scale=1.0))
            em.dmao(lambda e, fc=fc, tt=tt: e.dma_start(
                out=outs["gt"][fc*128:(fc+1)*128, tt*512:(tt+1)*512], in_=ot[:, :]))
    em.dma(lambda e: e.dma_start(out=wbuf[:, :8*E], in_=gw[:, :]))
    for tt in range(TPC // 128):
        for dc in range(8):
            em.pe(lambda e, tt=tt, dc=dc: e.matmul(
                out=pl[:, :], lhsT=xs[dc][:, tt*128:(tt+1)*128],
                rhs=wbuf[:, dc*E:(dc+1)*E], start=(dc == 0), stop=(dc == 7)))
        em.sc(lambda e: e.activation(out=ot[:, :E], in_=pl[:, :], func=AF.Copy, scale=-20.0))
        em.dmao(lambda e, tt=tt: e.dma_start(out=la_o[tt*128:(tt+1)*128, :], in_=ot[:, :E]))
    em.fin()
    em.build(nc, st)
    return nc


def build_scan():
    nc = bass.Bass("TRN2", target_bir_lowering=False)
    st = ExitStack()
    qi = nc.dram_tensor("q", [4*64, CH], F32, kind="ExternalInput").ap()
    ki = nc.dram_tensor("k", [4*64, CH], F32, kind="ExternalInput").ap()
    vi = nc.dram_tensor("v", [4*64, CH], F32, kind="ExternalInput").ap()
    ai = nc.dram_tensor("a", [4*64, CH], F32, kind="ExternalInput").ap()
    bi = nc.dram_tensor("b", [4, CH], F32, kind="ExternalInput").ap()
    si = nc.dram_tensor("si", [4*64, DV], F32, kind="ExternalInput").ap()
    idm = nc.dram_tensor("idm", [64, 64], F32, kind="ExternalInput").ap()
    so = nc.dram_tensor("so", [4*64, DV], F32, kind="ExternalOutput").ap()
    yo = nc.dram_tensor("y", [4*CH, DV], F32, kind="ExternalOutput").ap()
    em = Em()
    qb = st.enter_context(nc.sbuf_tensor("qb", [64, CH], F32))
    kb = st.enter_context(nc.sbuf_tensor("kb", [64, CH], F32))
    vb = st.enter_context(nc.sbuf_tensor("vb", [64, CH], F32))
    ab = st.enter_context(nc.sbuf_tensor("ab", [64, CH], F32))
    kn = st.enter_context(nc.sbuf_tensor("kn", [64, CH], F32))
    br = st.enter_context(nc.sbuf_tensor("br", [1, CH], F32))
    idt = st.enter_context(nc.sbuf_tensor("idt", [64, 64], F32))
    Sa = st.enter_context(nc.sbuf_tensor("Sa", [64, DV], F32))
    Sc = st.enter_context(nc.sbuf_tensor("Sc", [64, DV], F32))
    kM = st.enter_context(nc.sbuf_tensor("kM", [CH, 64], F32))
    vM = st.enter_context(nc.sbuf_tensor("vM", [CH, 64], F32))
    wr = st.enter_context(nc.sbuf_tensor("wr", [1, 64], F32))
    kr = st.enter_context(nc.sbuf_tensor("kr", [1, 64], F32))
    orow = st.enter_context(nc.sbuf_tensor("orow", [1, 64], F32))
    pw = st.enter_context(nc.psum_tensor("pw", [1, 64], F32))
    pko = st.enter_context(nc.psum_tensor("pko", [64, 64], F32))
    prow = st.enter_context(nc.psum_tensor("prow", [1, 64], F32))
    pout = st.enter_context(nc.psum_tensor("pout", [1, 64], F32))
    em.dma(lambda e: e.dma_start(out=idt[:, :], in_=idm[:, :]))
    for p in range(4):
        r0 = p * 64
        em.dma(lambda e, r=r0: e.dma_start(out=qb[:, :], in_=qi[r:r+64, :]))
        em.dma(lambda e, r=r0: e.dma_start(out=kb[:, :], in_=ki[r:r+64, :]))
        em.dma(lambda e, r=r0: e.dma_start(out=vb[:, :], in_=vi[r:r+64, :]))
        em.dma(lambda e, r=r0: e.dma_start(out=ab[:, :], in_=ai[r:r+64, :]))
        em.dma(lambda e, p=p: e.dma_start(out=br[:, :], in_=bi[p:p+1, :]))
        em.dma(lambda e, r=r0: e.dma_start(out=Sa[:, :], in_=si[r:r+64, :]))
        em.sc(lambda e: e.mul(out=kn[:, :], in_=kb[:, :], mul=-1.0))
        em.pe(lambda e: e.transpose(out=pko[:, :], in_=kb[:, :], identity=idt[:, :]))
        em.ve(lambda e: e.tensor_copy(out=kM[:, :], in_=pko[:, :]))
        em.pe(lambda e: e.transpose(out=pko[:, :], in_=vb[:, :], identity=idt[:, :]))
        em.ve(lambda e: e.tensor_copy(out=vM[:, :], in_=pko[:, :]))
        cur, nxt = Sa, Sc
        for t in range(CH):
            em.ve(lambda e, t=t, c=cur, n=nxt: e.tensor_scalar_mul(
                out=n[:, :], in0=c[:, :], scalar1=ab[:, t:t+1]))
            em.pe(lambda e, t=t: e.matmul(out=pw[:, :], lhsT=idt[:, t:t+1], rhs=vM[:, :],
                                          start=True, stop=False))
            em.pe(lambda e, t=t, n=nxt: e.matmul(out=pw[:, :], lhsT=kn[:, t:t+1], rhs=n[:, :],
                                                 start=False, stop=True))
            em.ve(lambda e, t=t: e.tensor_scalar_mul(out=wr[:, :], in0=pw[:, :],
                                                     scalar1=br[:, t:t+1]))
            em.pe(lambda e, t=t: e.matmul(out=prow[:, :], lhsT=idt[:, t:t+1], rhs=kM[:, :],
                                          start=True, stop=True))
            em.ve(lambda e: e.tensor_copy(out=kr[:, :], in_=prow[:, :]))
            em.pe(lambda e: e.matmul(out=pko[:, :], lhsT=kr[:, :], rhs=wr[:, :],
                                     start=True, stop=True))
            em.ve(lambda e, n=nxt: e.tensor_add(out=n[:, :], in0=n[:, :], in1=pko[:, :]))
            em.pe(lambda e, t=t, n=nxt: e.matmul(out=pout[:, :], lhsT=qb[:, t:t+1], rhs=n[:, :],
                                                 start=True, stop=True))
            em.ve(lambda e: e.tensor_copy(out=orow[:, :], in_=pout[:, :]))
            em.dmao(lambda e, p=p, t=t: e.dma_start(out=yo[p*CH+t:p*CH+t+1, :], in_=orow[:, :]))
            cur, nxt = nxt, cur
        em.dmao(lambda e, r=r0, c=cur: e.dma_start(out=so[r:r+64, :], in_=c[:, :]))
    em.fin()
    em.build(nc, st)
    return nc


def build_sink():
    nc = bass.Bass("TRN2", target_bir_lowering=False)
    st = ExitStack()
    la = nc.dram_tensor("la", [128, 128 * E], F32, kind="ExternalInput").ap()
    oa = nc.dram_tensor("oa", [128, 1], F32, kind="ExternalInput").ap()
    ob = nc.dram_tensor("ob", [1, 128], F32, kind="ExternalInput").ap()
    lo = nc.dram_tensor("lo", [128, 128 * E], F32, kind="ExternalOutput").ap()
    em = Em()
    lab = st.enter_context(nc.sbuf_tensor("lab", [128, 128 * E], F32))
    Eb = st.enter_context(nc.sbuf_tensor("Eb", [128, 128 * E], F32))
    lam = st.enter_context(nc.sbuf_tensor("lam", [128, E], F32))
    et = st.enter_context(nc.sbuf_tensor("et", [128, E], F32))
    mm = st.enter_context(nc.sbuf_tensor("mm", [128, 1], F32))
    ss = st.enter_context(nc.sbuf_tensor("ss", [128, 1], F32))
    csr = st.enter_context(nc.sbuf_tensor("csr", [1, 1024], F32))
    bcs = st.enter_context(nc.sbuf_tensor("bcs", [128, E], F32))
    o1 = st.enter_context(nc.sbuf_tensor("o1", [128, 1], F32))
    o2 = st.enter_context(nc.sbuf_tensor("o2", [1, 128], F32))
    psc = st.enter_context(nc.psum_tensor("psc", [1, 512], F32))
    psb = st.enter_context(nc.psum_tensor("psb", [128, E], F32))
    em.dma(lambda e: e.dma_start(out=o1[:, :], in_=oa[:, :]))
    em.dma(lambda e: e.dma_start(out=o2[:, :], in_=ob[:, :]))
    em.dma(lambda e: e.dma_start(out=lab[:, :], in_=la[:, :]))
    for j in range(128):
        sl = lab[:, j*E:(j+1)*E]
        em.ve(lambda e, sl=sl: e.reduce_max(out=mm[:, :], in_=sl, axis=AX.X))
        em.sc(lambda e: e.mul(out=mm[:, :], in_=mm[:, :], mul=-1.0))
        em.ve(lambda e, sl=sl: e.tensor_scalar_add(out=lam[:, :], in0=sl, scalar1=mm[:, :]))
        em.sc(lambda e: e.activation(out=et[:, :], in_=lam[:, :], func=AF.Exp))
        em.ve(lambda e: e.reduce_sum(out=ss[:, :], in_=et[:, :], axis=AX.X))
        em.sc(lambda e: e.activation(out=ss[:, :], in_=ss[:, :], func=AF.Ln))
        em.sc(lambda e: e.mul(out=ss[:, :], in_=ss[:, :], mul=-1.0))
        em.ve(lambda e, sl=sl: e.tensor_scalar_add(out=sl, in0=lam[:, :], scalar1=ss[:, :]))
    em.sc(lambda e: e.activation(out=Eb[:, :], in_=lab[:, :], func=AF.Exp))
    for hh in range(2):
        em.pe(lambda e, hh=hh: e.matmul(out=psc[:, :], lhsT=o1[:, :],
                                        rhs=Eb[:, hh*512:(hh+1)*512], start=True, stop=True))
        em.ve(lambda e, hh=hh: e.tensor_copy(out=csr[:, hh*512:(hh+1)*512], in_=psc[:, :]))
    wdt = 512
    while wdt >= E:
        em.ve(lambda e, w=wdt: e.tensor_add(out=csr[:, :w], in0=csr[:, :w], in1=csr[:, w:2*w]))
        wdt //= 2
    em.sc(lambda e: e.activation(out=csr[:, :E], in_=csr[:, :E], func=AF.Ln))
    em.sc(lambda e: e.mul(out=csr[:, :E], in_=csr[:, :E], mul=-1.0))
    em.pe(lambda e: e.matmul(out=psb[:, :], lhsT=o2[:, :], rhs=csr[:, :E], start=True, stop=True))
    em.ve(lambda e: e.tensor_copy(out=bcs[:, :], in_=psb[:, :]))
    for j in range(128):
        em.ve(lambda e, j=j: e.tensor_add(out=lab[:, j*E:(j+1)*E], in0=lab[:, j*E:(j+1)*E],
                                          in1=bcs[:, :]))
    em.dmao(lambda e: e.dma_start(out=lo[:, :], in_=lab[:, :]))
    em.fin()
    em.build(nc, st)
    return nc


def build_top2():
    nc = bass.Bass("TRN2", target_bir_lowering=False)
    st = ExitStack()
    la = nc.dram_tensor("la", [TPC, E], F32, kind="ExternalInput").ap()
    w8 = nc.dram_tensor("w8", [TPC, E], F32, kind="ExternalOutput").ap()
    em = Em()
    lt = st.enter_context(nc.sbuf_tensor("lt", [128, E], F32))
    pi = st.enter_context(nc.sbuf_tensor("pi", [128, E], F32))
    mk1 = st.enter_context(nc.sbuf_tensor("mk1", [128, E], F32))
    mk2 = st.enter_context(nc.sbuf_tensor("mk2", [128, E], F32))
    pi2 = st.enter_context(nc.sbuf_tensor("pi2", [128, E], F32))
    big = st.enter_context(nc.sbuf_tensor("big", [128, E], F32))
    m1 = st.enter_context(nc.sbuf_tensor("m1", [128, 1], F32))
    wt = st.enter_context(nc.sbuf_tensor("wt", [128, E], F32))
    for i in range(TPC // 128):
        r0 = i * 128
        em.dma(lambda e, r=r0: e.dma_start(out=lt[:, :], in_=la[r:r+128, :]))
        em.sc(lambda e: e.activation(out=pi[:, :], in_=lt[:, :], func=AF.Exp))
        em.ve(lambda e: e.reduce_max(out=m1[:, :], in_=pi[:, :], axis=AX.X))
        em.ve(lambda e: e.tensor_scalar(out=mk1[:, :], in0=pi[:, :], scalar1=m1[:, :],
                                        scalar2=None, op0=ALU.is_equal))
        em.sc(lambda e: e.mul(out=big[:, :], in_=mk1[:, :], mul=1e30))
        em.ve(lambda e: e.tensor_sub(out=pi2[:, :], in0=pi[:, :], in1=big[:, :]))
        em.ve(lambda e: e.reduce_max(out=m1[:, :], in_=pi2[:, :], axis=AX.X))
        em.ve(lambda e: e.tensor_scalar(out=mk2[:, :], in0=pi2[:, :], scalar1=m1[:, :],
                                        scalar2=None, op0=ALU.is_equal))
        em.ve(lambda e: e.tensor_scalar_mul(out=mk1[:, :], in0=mk1[:, :], scalar1=pi[:, 0:1]))
        em.ve(lambda e: e.tensor_scalar_mul(out=mk2[:, :], in0=mk2[:, :], scalar1=pi[:, 1:2]))
        em.ve(lambda e: e.tensor_add(out=wt[:, :], in0=mk1[:, :], in1=mk2[:, :]))
        em.dmao(lambda e, r=r0: e.dma_start(out=w8[r:r+128, :], in_=wt[:, :]))
    em.fin()
    em.build(nc, st)
    return nc


def build_mlp(final):
    nc = bass.Bass("TRN2", target_bir_lowering=False)
    st = ExitStack()
    xin = nc.dram_tensor("xin", [D, TPC], F32, kind="ExternalInput").ap()
    wg = nc.dram_tensor("wg", [128, 8, FF], F32, kind="ExternalInput").ap()
    wu = nc.dram_tensor("wu", [128, 8, FF], F32, kind="ExternalInput").ap()
    wd = nc.dram_tensor("wd", [FF, D], F32, kind="ExternalInput").ap()
    if final:
        y1 = nc.dram_tensor("y1", [TPC, D], F32, kind="ExternalInput").ap()
        xo = nc.dram_tensor("x", [TPC, D], F32, kind="ExternalInput").ap()
        mo = nc.dram_tensor("mo", [TPC, D], F32, kind="ExternalInput").ap()
        out_o = nc.dram_tensor("fin", [TPC, D], F32, kind="ExternalOutput").ap()
    else:
        wc = nc.dram_tensor("wc", [128, 16], F32, kind="ExternalInput").ap()
        acc = nc.dram_tensor("acc", [TPC, D], F32, kind="ExternalInput").ap()
        out_o = nc.dram_tensor("ao", [TPC, D], F32, kind="ExternalOutput").ap()
    em = Em()
    xsb = [st.enter_context(nc.sbuf_tensor(f"xs{i}", [128, TPC], F32)) for i in range(8)]
    ats = st.enter_context(nc.sbuf_tensor("ats", [128, 32 * 512], F32))
    wgb = st.enter_context(nc.sbuf_tensor("wgb", [128, 8 * 128], F32))
    wub = st.enter_context(nc.sbuf_tensor("wub", [128, 8 * 128], F32))
    wdb = st.enter_context(nc.sbuf_tensor("wdb", [128, D], F32))
    gst = st.enter_context(nc.sbuf_tensor("gst", [128, 512], F32))
    aux = st.enter_context(nc.sbuf_tensor("aux", [128, D], F32))
    aux2 = st.enter_context(nc.sbuf_tensor("aux2", [128, D], F32))
    fint = st.enter_context(nc.sbuf_tensor("fint", [128, D], F32))
    if not final:
        wct = st.enter_context(nc.sbuf_tensor("wct", [128, 16], F32))
    psg = st.enter_context(nc.psum_tensor("psg", [128, 512], F32))
    psu = st.enter_context(nc.psum_tensor("psu", [128, 512], F32))
    psd = [st.enter_context(nc.psum_tensor(f"psd{i}", [128, 512], F32)) for i in range(2)]
    for i in range(8):
        em.dma(lambda e, i=i: e.dma_start(out=xsb[i][:, :], in_=xin[i*128:(i+1)*128, :]))
    if not final:
        em.dma(lambda e: e.dma_start(out=wct[:, :], in_=wc[:, :]))
    for q in range(4):
        c0 = q * 512
        for fc in range(32):
            for dc in range(8):
                em.dma(lambda e, fc=fc, dc=dc: e.dma_start(
                    out=wgb[:, dc*128:(dc+1)*128], in_=wg[:, dc, fc*128:(fc+1)*128]))
                em.dma(lambda e, fc=fc, dc=dc: e.dma_start(
                    out=wub[:, dc*128:(dc+1)*128], in_=wu[:, dc, fc*128:(fc+1)*128]))
            for dc in range(8):
                em.pe(lambda e, dc=dc, c=c0: e.matmul(
                    out=psg[:, :], lhsT=wgb[:, dc*128:(dc+1)*128], rhs=xsb[dc][:, c:c+512],
                    start=(dc == 0), stop=(dc == 7)))
            em.sc(lambda e: e.activation(out=gst[:, :], in_=psg[:, :], func=AF.Silu))
            for dc in range(8):
                em.pe(lambda e, dc=dc, c=c0: e.matmul(
                    out=psu[:, :], lhsT=wub[:, dc*128:(dc+1)*128], rhs=xsb[dc][:, c:c+512],
                    start=(dc == 0), stop=(dc == 7)))
            em.ve(lambda e, fc=fc: e.tensor_mul(out=ats[:, fc*512:(fc+1)*512],
                                                in0=gst[:, :], in1=psu[:, :]))
        for tt in range(4):
            gidx = q * 4 + tt
            for fc in range(32):
                em.dma(lambda e, fc=fc: e.dma_start(out=wdb[:, :], in_=wd[fc*128:(fc+1)*128, :]))
                for dh in range(2):
                    em.pe(lambda e, fc=fc, tt=tt, dh=dh: e.matmul(
                        out=psd[dh][:, :], lhsT=ats[:, fc*512 + tt*128: fc*512 + (tt+1)*128],
                        rhs=wdb[:, dh*512:(dh+1)*512], start=(fc == 0), stop=(fc == 31)))
            if final:
                em.dma(lambda e, g=gidx: e.dma_start(out=aux[:, :], in_=y1[g*128:(g+1)*128, :]))
                em.dma(lambda e, g=gidx: e.dma_start(out=aux2[:, :], in_=xo[g*128:(g+1)*128, :]))
                for dh in range(2):
                    em.ve(lambda e, dh=dh: e.tensor_add(
                        out=fint[:, dh*512:(dh+1)*512], in0=aux[:, dh*512:(dh+1)*512],
                        in1=psd[dh][:, :]))
                em.ve(lambda e: e.tensor_add(out=fint[:, :], in0=fint[:, :], in1=aux2[:, :]))
                em.dma(lambda e, g=gidx: e.dma_start(out=aux[:, :], in_=mo[g*128:(g+1)*128, :]))
                em.ve(lambda e: e.tensor_add(out=fint[:, :], in0=fint[:, :], in1=aux[:, :]))
            else:
                em.dma(lambda e, g=gidx: e.dma_start(out=aux[:, :], in_=acc[g*128:(g+1)*128, :]))
                for dh in range(2):
                    em.ve(lambda e, dh=dh, g=gidx: e.tensor_scalar_mul(
                        out=fint[:, dh*512:(dh+1)*512], in0=psd[dh][:, :],
                        scalar1=wct[:, g:g+1]))
                em.ve(lambda e: e.tensor_add(out=fint[:, :], in0=fint[:, :], in1=aux[:, :]))
            em.dmao(lambda e, g=gidx: e.dma_start(out=out_o[g*128:(g+1)*128, :], in_=fint[:, :]))
    em.fin()
    em.build(nc, st)
    return nc


def build_hnorm():
    nc = bass.Bass("TRN2", target_bir_lowering=False)
    st = ExitStack()
    y = nc.dram_tensor("y", [TPC, D], F32, kind="ExternalInput").ap()
    hw = nc.dram_tensor("hw", [1, D], F32, kind="ExternalInput").ap()
    o1 = nc.dram_tensor("o1", [1, 128], F32, kind="ExternalInput").ap()
    ep = nc.dram_tensor("ep", [128, 2], F32, kind="ExternalInput").ap()
    yn = nc.dram_tensor("yn", [TPC, D], F32, kind="ExternalOutput").ap()
    em = Em()
    yt = st.enter_context(nc.sbuf_tensor("yt", [128, D], F32))
    ot = st.enter_context(nc.sbuf_tensor("ot", [128, D], F32))
    sq = st.enter_context(nc.sbuf_tensor("sq", [128, 64], F32))
    m1 = st.enter_context(nc.sbuf_tensor("m1", [128, 1], F32))
    hb = st.enter_context(nc.sbuf_tensor("hb", [128, D], F32))
    hrow = st.enter_context(nc.sbuf_tensor("hrow", [1, D], F32))
    ones = st.enter_context(nc.sbuf_tensor("ones", [1, 128], F32))
    ept = st.enter_context(nc.sbuf_tensor("ept", [128, 2], F32))
    psb = st.enter_context(nc.psum_tensor("psb", [128, 512], F32))
    em.dma(lambda e: e.dma_start(out=hrow[:, :], in_=hw[:, :]))
    em.dma(lambda e: e.dma_start(out=ones[:, :], in_=o1[:, :]))
    em.dma(lambda e: e.dma_start(out=ept[:, :], in_=ep[:, :]))
    _bcast_row(em, nc, psb, ones, hrow, hb, D)
    for i in range(TPC // 128):
        r0 = i * 128
        em.dma(lambda e, r=r0: e.dma_start(out=yt[:, :], in_=y[r:r+128, :]))
        for h in range(H):
            sl = yt[:, h*64:(h+1)*64]
            em.ve(lambda e, sl=sl: e.tensor_mul(out=sq[:, :], in0=sl, in1=sl))
            em.ve(lambda e: e.reduce_sum(out=m1[:, :], in_=sq[:, :], axis=AX.X))
            em.sc(lambda e: e.mul(out=m1[:, :], in_=m1[:, :], mul=1.0 / 64))
            em.ve(lambda e: e.tensor_scalar_add(out=m1[:, :], in0=m1[:, :], scalar1=ept[:, 1:2]))
            em.sc(lambda e: e.activation(out=m1[:, :], in_=m1[:, :], func=AF.Sqrt))
            em.ve(lambda e: e.reciprocal(out=m1[:, :], in_=m1[:, :]))
            em.ve(lambda e, sl=sl, h=h: e.tensor_scalar_mul(
                out=ot[:, h*64:(h+1)*64], in0=sl, scalar1=m1[:, :]))
        em.ve(lambda e: e.tensor_mul(out=ot[:, :], in0=ot[:, :], in1=hb[:, :]))
        em.dmao(lambda e, r=r0: e.dma_start(out=yn[r:r+128, :], in_=ot[:, :]))
    em.fin()
    em.build(nc, st)
    return nc


def build_LA():
    nc = bass.Bass("TRN2", target_bir_lowering=False)
    st = ExitStack()
    ynT = nc.dram_tensor("ynT", [D, TPC], F32, kind="ExternalInput").ap()
    ow = nc.dram_tensor("ow", [128, 8 * D], F32, kind="ExternalInput").ap()
    gate = nc.dram_tensor("gate", [TPC, D], F32, kind="ExternalInput").ap()
    xn = nc.dram_tensor("xn", [TPC, D], F32, kind="ExternalInput").ap()
    fw = nc.dram_tensor("fw", [1, D], F32, kind="ExternalInput").ap()
    o1 = nc.dram_tensor("o1", [1, 128], F32, kind="ExternalInput").ap()
    ep = nc.dram_tensor("ep", [128, 2], F32, kind="ExternalInput").ap()
    y1o = nc.dram_tensor("y1", [TPC, D], F32, kind="ExternalOutput").ap()
    zo = nc.dram_tensor("z", [TPC, D], F32, kind="ExternalOutput").ap()
    em = Em()
    ysb = [st.enter_context(nc.sbuf_tensor(f"ys{i}", [128, TPC], F32)) for i in range(8)]
    owb = st.enter_context(nc.sbuf_tensor("owb", [128, 8 * D], F32))
    gt = st.enter_context(nc.sbuf_tensor("gt", [128, D], F32))
    xt = st.enter_context(nc.sbuf_tensor("xt", [128, D], F32))
    y1t = st.enter_context(nc.sbuf_tensor("y1t", [128, D], F32))
    zt = st.enter_context(nc.sbuf_tensor("zt", [128, D], F32))
    sq = st.enter_context(nc.sbuf_tensor("sq", [128, D], F32))
    m1 = st.enter_context(nc.sbuf_tensor("m1", [128, 1], F32))
    fb = st.enter_context(nc.sbuf_tensor("fb", [128, D], F32))
    frow = st.enter_context(nc.sbuf_tensor("frow", [1, D], F32))
    ones = st.enter_context(nc.sbuf_tensor("ones", [1, 128], F32))
    ept = st.enter_context(nc.sbuf_tensor("ept", [128, 2], F32))
    ph = [st.enter_context(nc.psum_tensor(f"ph{i}", [128, 512], F32)) for i in range(2)]
    em.dma(lambda e: e.dma_start(out=frow[:, :], in_=fw[:, :]))
    em.dma(lambda e: e.dma_start(out=ones[:, :], in_=o1[:, :]))
    em.dma(lambda e: e.dma_start(out=ept[:, :], in_=ep[:, :]))
    _bcast_row(em, nc, ph[0], ones, frow, fb, D)
    em.dma(lambda e: e.dma_start(out=owb[:, :], in_=ow[:, :]))
    for i in range(8):
        em.dma(lambda e, i=i: e.dma_start(out=ysb[i][:, :], in_=ynT[i*128:(i+1)*128, :]))
    for g in range(TPC // 128):
        r0 = g * 128
        for oh in range(2):
            for dc in range(8):
                em.pe(lambda e, g=g, oh=oh, dc=dc: e.matmul(
                    out=ph[oh][:, :], lhsT=ysb[dc][:, g*128:(g+1)*128],
                    rhs=owb[:, dc*D + oh*512: dc*D + oh*512 + 512],
                    start=(dc == 0), stop=(dc == 7)))
        em.dma(lambda e, r=r0: e.dma_start(out=gt[:, :], in_=gate[r:r+128, :]))
        em.dma(lambda e, r=r0: e.dma_start(out=xt[:, :], in_=xn[r:r+128, :]))
        for oh in range(2):
            em.ve(lambda e, oh=oh: e.tensor_mul(out=y1t[:, oh*512:(oh+1)*512],
                                                in0=gt[:, oh*512:(oh+1)*512], in1=ph[oh][:, :]))
        em.ve(lambda e: e.tensor_add(out=y1t[:, :], in0=y1t[:, :], in1=xt[:, :]))
        em.dmao(lambda e, r=r0: e.dma_start(out=y1o[r:r+128, :], in_=y1t[:, :]))
        em.ve(lambda e: e.tensor_mul(out=sq[:, :], in0=y1t[:, :], in1=y1t[:, :]))
        em.ve(lambda e: e.reduce_sum(out=m1[:, :], in_=sq[:, :], axis=AX.X))
        em.sc(lambda e: e.mul(out=m1[:, :], in_=m1[:, :], mul=1.0 / D))
        em.ve(lambda e: e.tensor_scalar_add(out=m1[:, :], in0=m1[:, :], scalar1=ept[:, 1:2]))
        em.sc(lambda e: e.activation(out=m1[:, :], in_=m1[:, :], func=AF.Sqrt))
        em.ve(lambda e: e.reciprocal(out=m1[:, :], in_=m1[:, :]))
        em.ve(lambda e: e.tensor_scalar_mul(out=zt[:, :], in0=y1t[:, :], scalar1=m1[:, :]))
        em.ve(lambda e: e.tensor_mul(out=zt[:, :], in0=zt[:, :], in1=fb[:, :]))
        em.dmao(lambda e, r=r0: e.dma_start(out=zo[r:r+128, :], in_=zt[:, :]))
    em.fin()
    em.build(nc, st)
    return nc


def _run(nc, maps):
    return run_bass_kernel_spmd(nc, maps, core_ids=list(range(NCORES)))


def kernel(**inp):
    f32 = lambda a: np.ascontiguousarray(np.asarray(a, dtype=np.float32))
    x = f32(inp["L0_attn_x"]).reshape(N, D)
    ones_a = np.ones((128, 1), np.float32)
    ones_b = np.ones((1, 128), np.float32)
    id64 = np.eye(64, dtype=np.float32)
    b2 = np.zeros((128, 2), np.float32); b2[:64, 0] = 1.0; b2[64:, 1] = 1.0
    i2 = np.zeros((2, 128), np.float32); i2[0, :64] = 1.0; i2[1, 64:] = 1.0
    epc = np.zeros((128, 2), np.float32); epc[:, 0] = 1e-5; epc[:, 1] = 1e-6

    sl = lambda c: slice(c * TPC, (c + 1) * TPC)

    nc1 = build_L1()
    w_in = f32(inp["L0_attn_norm_in_w"]).reshape(1, D)
    r = _run(nc1, [{"x": x[sl(c)], "w": w_in, "o1": ones_b, "ep": epc} for c in range(NCORES)])
    xn = np.concatenate([rr["xn"] for rr in r.results], axis=0)
    gfull = np.concatenate([rr["g"] for rr in r.results], axis=0)

    nc2 = build_L2()
    gT = np.ascontiguousarray(gfull.T)
    cw = f32(inp["L0_attn_conv_w"]).reshape(D, 4)
    cb = f32(inp["L0_attn_conv_b"]).reshape(D, 1)
    r = _run(nc2, [{"g": gT[c*128:(c+1)*128], "cw": cw[c*128:(c+1)*128],
                    "cb": cb[c*128:(c+1)*128]} for c in range(NCORES)])
    hT = np.concatenate([rr["h"] for rr in r.results], axis=0)
    xnT = np.ascontiguousarray(xn.T)

    nc3 = build_L3()
    pc = lambda a: np.ascontiguousarray(f32(a).reshape(8, 128).T)
    pre8 = lambda a: np.ascontiguousarray(
        f32(a).reshape(8, 128, -1).transpose(1, 0, 2).reshape(128, -1))
    maps = []
    for c in range(NCORES):
        maps.append({
            "h": np.ascontiguousarray(hT[:, sl(c)]), "xn": np.ascontiguousarray(xnT[:, sl(c)]),
            "qw": pre8(inp["L0_attn_q_w"]), "kw": pre8(inp["L0_attn_k_w"]),
            "vw": pre8(inp["L0_attn_v_w"]), "aw": pre8(inp["L0_attn_alpha_w"]),
            "ab": pc(inp["L0_attn_alpha_b"]), "bw": pre8(inp["L0_attn_beta_w"]),
            "bb": f32(inp["L0_attn_beta_b"]).reshape(16, 1),
            "u1w": pre8(inp["L0_attn_u1_w"]), "u1b": f32(inp["L0_attn_u1_b"]).reshape(64, 1),
            "u2w": f32(inp["L0_attn_u2_w"]), "u2b": pc(inp["L0_attn_u2_b"]),
            "gw": pre8(f32(inp["L0_ffn_gate_W"]).T),
            "b2": b2, "i2": i2, "ep": epc})
    r = _run(nc3, maps)
    qnT = np.concatenate([rr["qn"] for rr in r.results], axis=1)
    knT = np.concatenate([rr["kn"] for rr in r.results], axis=1)
    vT = np.concatenate([rr["vv"] for rr in r.results], axis=1)
    aT = np.concatenate([rr["aa"] for rr in r.results], axis=1)
    gateT = np.concatenate([rr["gt"] for rr in r.results], axis=1)
    bT = np.concatenate([rr["bt"] for rr in r.results], axis=1)
    la = np.concatenate([rr["la"] for rr in r.results], axis=0)

    ncS = build_scan()
    Sg = np.zeros((32, 64, DV), np.float32)
    y_tm = np.zeros((N, D), np.float32)
    for cc in range(NCHUNK):
        maps = []
        for c in range(NCORES):
            qq = []; kk = []; vv = []; aa = []; bb_ = []; ss = []
            for i in range(4):
                p = 4 * c + i
                b, h = p // 16, p % 16
                cs = slice(b * 8192 + cc * CH, b * 8192 + (cc + 1) * CH)
                qq.append(qnT[h*64:(h+1)*64, cs]); kk.append(knT[h*64:(h+1)*64, cs])
                vv.append(vT[h*64:(h+1)*64, cs]); aa.append(aT[h*64:(h+1)*64, cs])
                bb_.append(bT[h:h+1, cs]); ss.append(Sg[p])
            maps.append({"q": np.ascontiguousarray(np.concatenate(qq, 0)),
                         "k": np.ascontiguousarray(np.concatenate(kk, 0)),
                         "v": np.ascontiguousarray(np.concatenate(vv, 0)),
                         "a": np.ascontiguousarray(np.concatenate(aa, 0)),
                         "b": np.ascontiguousarray(np.concatenate(bb_, 0)),
                         "si": np.concatenate(ss, 0), "idm": id64})
        r = _run(ncS, maps)
        for c in range(NCORES):
            so = r.results[c]["so"]; yy = r.results[c]["y"]
            for i in range(4):
                p = 4 * c + i
                b, h = p // 16, p % 16
                Sg[p] = so[i*64:(i+1)*64]
                y_tm[b*8192 + cc*CH: b*8192 + (cc+1)*CH, h*64:(h+1)*64] = yy[i*CH:(i+1)*CH]

    ncK = build_sink()
    la_cur = np.ascontiguousarray(la.reshape(128, 128 * E))
    for _ in range(20):
        r = _run(ncK, [{"la": la_cur, "oa": ones_a, "ob": ones_b} for _ in range(NCORES)])
        la_cur = np.ascontiguousarray(r.results[0]["lo"])
    la_cur = la_cur.reshape(N, E)

    ncT = build_top2()
    r = _run(ncT, [{"la": np.ascontiguousarray(la_cur[sl(c)])} for c in range(NCORES)])
    w8 = np.concatenate([rr["w8"] for rr in r.results], axis=0)

    ncE = build_mlp(final=False)
    Wg = f32(inp["L0_ffn_W_gate"]); Wu = f32(inp["L0_ffn_W_up"]); Wd = f32(inp["L0_ffn_W_down"])
    acc = np.zeros((N, D), np.float32)
    xnT_sl = [np.ascontiguousarray(xnT[:, sl(c)]) for c in range(NCORES)]
    for e in range(E):
        wgT = np.ascontiguousarray(Wg[e].T.reshape(8, 128, FF).transpose(1, 0, 2))
        wuT = np.ascontiguousarray(Wu[e].T.reshape(8, 128, FF).transpose(1, 0, 2))
        wdT = np.ascontiguousarray(Wd[e].T)
        maps = []
        for c in range(NCORES):
            wcol = np.ascontiguousarray(w8[sl(c), e].reshape(16, 128).T)
            maps.append({"xin": xnT_sl[c], "wg": wgT, "wu": wuT, "wd": wdT,
                         "wc": wcol, "acc": acc[sl(c)]})
        r = _run(ncE, maps)
        acc = np.concatenate([rr["ao"] for rr in r.results], axis=0)

    ncH = build_hnorm()
    hw = f32(inp["L0_attn_head_norm_w"]).reshape(1, D)
    r = _run(ncH, [{"y": y_tm[sl(c)], "hw": hw, "o1": ones_b, "ep": epc} for c in range(NCORES)])
    yn = np.concatenate([rr["yn"] for rr in r.results], axis=0)
    ynT = np.ascontiguousarray(yn.T)

    ncA = build_LA()
    gate_tm = np.ascontiguousarray(gateT.T)
    fw = f32(inp["L0_attn_ff_norm_w"]).reshape(1, D)
    maps = [{"ynT": np.ascontiguousarray(ynT[:, sl(c)]), "ow": pre8(inp["L0_attn_out_w"]),
             "gate": gate_tm[sl(c)], "xn": xn[sl(c)], "fw": fw, "o1": ones_b, "ep": epc}
            for c in range(NCORES)]
    r = _run(ncA, maps)
    y1 = np.concatenate([rr["y1"] for rr in r.results], axis=0)
    z = np.concatenate([rr["z"] for rr in r.results], axis=0)
    zT = np.ascontiguousarray(z.T)

    ncB = build_mlp(final=True)
    w1p = np.ascontiguousarray(f32(inp["L0_attn_ff_w1"]).reshape(8, 128, FF).transpose(1, 0, 2))
    w3p = np.ascontiguousarray(f32(inp["L0_attn_ff_w3"]).reshape(8, 128, FF).transpose(1, 0, 2))
    maps = [{"xin": np.ascontiguousarray(zT[:, sl(c)]), "wg": w1p,
             "wu": w3p, "wd": f32(inp["L0_attn_ff_w2"]),
             "y1": y1[sl(c)], "x": x[sl(c)], "mo": acc[sl(c)]} for c in range(NCORES)]
    r = _run(ncB, maps)
    fin = np.concatenate([rr["fin"] for rr in r.results], axis=0)
    return fin.reshape(2, 8192, D)
